# version 02
# baseline (speedup 1.0000x reference)
import contextlib
import numpy as np

import concourse.bass as bass
import concourse.mybir as mybir
from concourse.bass_utils import run_bass_kernel_spmd

F32 = mybir.dt.float32
BF16 = mybir.dt.bfloat16
AF = mybir.ActivationFunctionType
ALU = mybir.AluOpType
AX = mybir.AxisListType

NCORES = 8
RACECHK = True
FULLDRAIN = False
PHASEC_DBG = False
QKV_NMAJOR = False
QKV_SWAP = False
B, S, H = 8, 2048, 1024
NH, HD = 16, 64
FC = H // 128
TOK = 2048
EPS = 1e-5


class Sem:
    def __init__(self, stack, nc, name):
        self.h = stack.enter_context(nc.semaphore(name))
        self.n = 0

    def inc(self, k=1):
        self.n += k
        return self.n


class Prog:
    ENGS = ("sync", "vector", "scalar", "tensor", "gpsimd")

    def __init__(self, nc, stack):
        self.nc = nc
        self.stack = stack
        self.ops = {e: [] for e in self.ENGS}

    def wait(self, eng, sem, val):
        if val is None or val <= 0:
            return
        self.ops[eng].append(lambda E, s=sem, v=val: E.wait_ge(s.h, v))

    def dma(self, eng, out, in_, sem, **kw):
        t = sem.inc(16)
        self.ops[eng].append(
            lambda E, o=out, i=in_, s=sem, kw=kw: E.dma_start(out=o, in_=i, **kw).then_inc(s.h, 16))
        return t

    def op(self, eng, fn, sem=None):
        if sem is None:
            self.ops[eng].append(fn)
            return None
        t = sem.inc(1)
        self.ops[eng].append(lambda E, f=fn, s=sem: f(E).then_inc(s.h, 1))
        return t

    def finalize(self):
        blk = self.stack.enter_context(self.nc.Block())
        for name in self.ENGS:
            lst = self.ops[name]
            getattr(blk, name)(lambda E, lst=lst: [f(E) for f in lst])


def _mm(out, lhsT, rhs, start, stop):
    return lambda E: E.matmul(out, lhsT, rhs, start=start, stop=stop)


def batched_n(ap2d, nsub=512):
    n = ap2d.shape[-1]
    if n <= nsub:
        return ap2d
    return ap2d.rearrange("p (b n) -> p b n", n=nsub)


class StagePipe:

    def __init__(self, p, bufs, ld_sem, cast_sem):
        self.p = p
        self.bufs = bufs
        self.ld = ld_sem
        self.cast = cast_sem
        self.hist = []

    def load_cast(self, src_ap, dst_ap, width, cast_eng):
        i = len(self.hist)
        sb = self.bufs[i % len(self.bufs)]
        if i >= len(self.bufs):
            self.p.wait("sync", self.cast, self.hist[i - len(self.bufs)])
        t_ld = self.p.dma("sync", sb[:, 0:width], src_ap, self.ld)
        self.p.wait(cast_eng, self.ld, t_ld)
        t = self.p.op(cast_eng, lambda E, sb=sb, w=width, d=dst_ap: E.tensor_copy(d, sb[:, 0:w]),
                      self.cast)
        self.hist.append(t)
        return t




def emit_ln_scalar(p, P0, P1, mu_t, aux_t, rstd_t, eps_t, onesf_row,
                   MM, EV, DV, stat_last):
    H_ = H
    p.wait("vector", MM, stat_last)
    p.op("vector", lambda E: E.tensor_scalar_mul(mu_t, P0[0:1, :], 1.0 / H_))
    t_var = p.op("vector", lambda E: E.tensor_scalar_mul(rstd_t, P1[0:1, :], 1.0 / H_), DV)
    p.wait("scalar", DV, t_var)
    t_mu2 = p.op("scalar", lambda E: E.activation(aux_t, mu_t, AF.Square), EV)
    p.wait("vector", EV, t_mu2)
    t_v2 = p.op("vector", lambda E: E.tensor_sub(rstd_t, rstd_t, aux_t), DV)
    p.wait("scalar", DV, t_v2)
    t_lnr = p.op("scalar", lambda E: E.activation(rstd_t, rstd_t, AF.Ln, bias=eps_t[:]), EV)
    p.wait("scalar", EV, t_lnr)
    t_rstd = p.op("scalar", lambda E: E.activation(rstd_t, rstd_t, AF.Exp, scale=-0.5), EV)
    p.wait("vector", EV, t_rstd)
    t_bofs = p.op("vector", lambda E: E.scalar_tensor_tensor(
        aux_t, mu_t, -1.0, rstd_t, ALU.mult, ALU.mult), DV)
    p.wait("tensor", DV, t_bofs)
    tB2 = None
    for n in range(4):
        p.op("tensor", _mm(P0[:, n * 512:(n + 1) * 512], onesf_row[:],
                           rstd_t[:, n * 512:(n + 1) * 512], start=True, stop=True))
        tB2 = p.op("tensor", _mm(P1[:, n * 512:(n + 1) * 512], onesf_row[:],
                                 aux_t[:, n * 512:(n + 1) * 512], start=True, stop=True), MM)
    return tB2


def build_phaseB():
    nc = bass.Bass("TRN2", target_bir_lowering=False, detect_race_conditions=RACECHK)
    x1s = nc.dram_tensor("x1s", [H, TOK], F32, kind="ExternalInput").ap()
    wsT = nc.dram_tensor("wsT", [H, 3 * H], F32, kind="ExternalInput").ap()
    usT = nc.dram_tensor("usT", [H, 3 * H], F32, kind="ExternalInput").ap()
    woT = nc.dram_tensor("woT", [H, H], F32, kind="ExternalInput").ap()
    uoT = nc.dram_tensor("uoT", [H, H], F32, kind="ExternalInput").ap()
    bsrow = nc.dram_tensor("bsrow", [1, 3 * H], F32, kind="ExternalInput").ap()
    bo_pp = nc.dram_tensor("bo_pp", [128, FC], F32, kind="ExternalInput").ap()
    inc_pp = nc.dram_tensor("inc_pp", [128, FC], F32, kind="ExternalInput").ap()
    x2s = nc.dram_tensor("x2s", [H, TOK], F32, kind="ExternalOutput").ap()

    with contextlib.ExitStack() as stack:
        ent = stack.enter_context
        p = Prog(nc, stack)
        ones_col = ent(nc.sbuf_tensor("ones_col", [128, 1], BF16))
        ones_row = ent(nc.sbuf_tensor("ones_row", [1, 512], BF16))
        onesf_row = ent(nc.sbuf_tensor("onesf_row", [1, 128], F32))
        ones_s = ent(nc.sbuf_tensor("ones_s", [128, 256], F32))
        bo_s = ent(nc.sbuf_tensor("bo_s", [128, FC], F32))
        eps_t = ent(nc.sbuf_tensor("eps_t", [1, 1], F32))
        inc_s = ent(nc.sbuf_tensor("inc_s", [128, FC], F32))
        c32 = ent(nc.sbuf_tensor("c32", [128, FC * 256], F32))
        cb = ent(nc.sbuf_tensor("cb", [128, FC * 256], BF16))
        stage = [ent(nc.sbuf_tensor(f"stage{i}", [128, 2048], F32)) for i in range(2)]
        uocS = ent(nc.sbuf_tensor("uocS", [128, TOK], F32))
        t3 = ent(nc.sbuf_tensor("t3", [128, TOK], F32))
        x1b = ent(nc.sbuf_tensor("x1b", [128, FC * TOK], BF16))
        P0 = ent(nc.psum_tensor("P0", [128, 2048], F32))
        P1 = ent(nc.psum_tensor("P1", [128, 2048], F32))

        LD = Sem(stack, nc, "LD")
        CAST = Sem(stack, nc, "CAST")
        MM = Sem(stack, nc, "MM")
        EV = Sem(stack, nc, "EV")
        DV = Sem(stack, nc, "DV")
        STS = Sem(stack, nc, "STS")

        pipe = StagePipe(p, stage, LD, CAST)

        t_small = 0
        for (dst, src) in ((bo_s[:], bo_pp[:, :]), (inc_s[:], inc_pp[:, :])):
            t_small = p.dma("sync", dst, src, LD)
        p.wait("vector", LD, t_small)
        p.op("vector", lambda E: E.memset(ones_col[:], 1.0))
        p.op("vector", lambda E: E.memset(ones_row[:], 1.0))
        p.op("vector", lambda E: E.memset(onesf_row[:], 1.0))
        p.op("vector", lambda E: E.memset(ones_s[:], 1.0))
        p.op("vector", lambda E: E.memset(eps_t[:], float(EPS)))
        for c in range(FC):
            p.op("vector", lambda E, c=c: E.tensor_scalar_mul(
                c32[:, c * 256:(c + 1) * 256], ones_s[:], inc_s[:, c:c + 1]))
        t_c0 = p.op("vector", lambda E: E.tensor_copy(cb[:], c32[:]), DV)

        for fc in range(FC):
            t_x1b = pipe.load_cast(x1s[fc * 128:(fc + 1) * 128, :],
                                   x1b[:, fc * TOK:(fc + 1) * TOK], TOK, "vector")

        stW = contextlib.ExitStack()
        wsb = stW.enter_context(nc.sbuf_tensor("wsb", [128, FC * 3 * H], BF16))
        usb = stW.enter_context(nc.sbuf_tensor("usb", [128, FC * 3 * H], BF16))
        fsb = stW.enter_context(nc.sbuf_tensor("fsb", [128, TOK], BF16))
        isb = stW.enter_context(nc.sbuf_tensor("isb", [128, TOK], BF16))
        gsb = stW.enter_context(nc.sbuf_tensor("gsb", [128, TOK], BF16))
        bs_f = stW.enter_context(nc.sbuf_tensor("bs_f", [1, 3 * H], F32))
        bs_bf = stW.enter_context(nc.sbuf_tensor("bs_bf", [1, 3 * H], BF16))

        def load_weightT(src_ap, dst_sb, cols):
            last = None
            gran = 2048
            for k in range(FC):
                for c0 in range(0, cols, gran):
                    w = min(gran, cols - c0)
                    last = pipe.load_cast(src_ap[k * 128:(k + 1) * 128, c0:c0 + w],
                                          dst_sb[:, k * cols + c0:k * cols + c0 + w],
                                          w, "gpsimd")
            return last

        t_bs = p.dma("sync", bs_f[:], bsrow[:, :], LD)
        p.wait("vector", LD, t_bs)
        t_bsc = p.op("vector", lambda E: E.tensor_copy(bs_bf[:], bs_f[:]), DV)
        p.wait("tensor", DV, t_bsc)
        t_ws = load_weightT(wsT, wsb, 3 * H)
        t_us = load_weightT(usT, usb, 3 * H)

        AFg = [AF.Sigmoid, AF.Sigmoid, AF.Tanh]
        gate_out = [fsb, isb, gsb]
        PP = [P0, P1]
        ev_t = {}
        cb_t = {}
        p.wait("tensor", CAST, t_us)
        p.wait("tensor", DV, t_c0)
        for t in range(B):
            for gi in range(3):
                ps = PP[gi % 2]
                prev = {0: (t - 1, 2), 1: (t - 1, 1), 2: (t, 0)}[gi]
                if prev in ev_t:
                    p.wait("tensor", EV, ev_t[prev])
                if gi == 0 and t > 0:
                    p.wait("tensor", DV, cb_t[t - 1])
                last_mm = None
                for m in range(8):
                    dst = ps[:, m * 256:(m + 1) * 256]
                    base = gi * H + m * 128
                    for k in range(FC):
                        p.op("tensor", _mm(
                            dst, wsb[:, k * 3 * H + base: k * 3 * H + base + 128],
                            x1b[:, k * TOK + t * 256: k * TOK + t * 256 + 256],
                            start=(k == 0), stop=False))
                    for k in range(FC):
                        p.op("tensor", _mm(
                            dst, usb[:, k * 3 * H + base: k * 3 * H + base + 128],
                            cb[:, k * 256:(k + 1) * 256],
                            start=False, stop=False))
                    last_mm = p.op("tensor", _mm(
                        dst, bs_bf[:, base: base + 128],
                        ones_row[:, 0:256], start=False, stop=True), MM)
                p.wait("scalar", MM, last_mm)
                ev_t[(t, gi)] = p.op("scalar", lambda E, ps=ps, gi=gi: E.activation(
                    gate_out[gi][:], ps[:], AFg[gi]), EV)
            p.wait("vector", EV, ev_t[(t, 2)])
            p.op("vector", lambda E: E.tensor_mul(t3[:], isb[:], gsb[:]))
            p.op("vector", lambda E: E.tensor_mul(c32[:], fsb[:], c32[:]))
            p.op("vector", lambda E: E.tensor_add(c32[:], c32[:], t3[:]))
            cb_t[t] = p.op("vector", lambda E: E.tensor_copy(cb[:], c32[:]), DV)

        stW.close()
        h32 = ent(nc.sbuf_tensor("h32", [128, FC * TOK], F32))
        wob = ent(nc.sbuf_tensor("wob", [128, FC * H], BF16))
        uob = ent(nc.sbuf_tensor("uob", [128, FC * H], BF16))
        p.wait("sync", EV, ev_t[(B - 1, 2)])
        t_wo = load_weightT(woT, wob, H)
        t_uo = load_weightT(uoT, uob, H)

        p.wait("tensor", CAST, t_uo)
        p.wait("tensor", DV, cb_t[B - 1])
        last = None
        for m in range(8):
            for k in range(FC):
                last = p.op("tensor", _mm(
                    P0[:, m * 256:(m + 1) * 256],
                    uob[:, k * H + m * 128: k * H + m * 128 + 128],
                    cb[:, k * 256:(k + 1) * 256],
                    start=(k == 0), stop=(k == FC - 1)), MM)
        p.wait("vector", MM, last)
        t_uoc = p.op("vector", lambda E: E.tensor_copy(uocS[:], P0[:]), DV)

        h_done = []
        h_t3 = {}
        for m in range(8):
            ps = PP[m % 2]
            if m == 0:
                p.wait("tensor", DV, t_uoc)
            if m >= 2:
                p.wait("tensor", DV, h_t3[m - 2])
            last = None
            for k in range(FC):
                for n in range(4):
                    last = p.op("tensor", _mm(
                        ps[:, n * 512:(n + 1) * 512],
                        wob[:, k * H + m * 128: k * H + m * 128 + 128],
                        x1b[:, k * TOK + n * 512: k * TOK + n * 512 + 512],
                        start=(k == 0), stop=(k == FC - 1)), MM)
            p.wait("vector", MM, last)
            if m == 0:
                p.wait("vector", DV, t_uoc)
            tlast = None
            for t in range(B):
                fn = lambda E, ps=ps, t=t, m=m: E.tensor_add(
                    t3[:, t * 256:(t + 1) * 256],
                    ps[:, t * 256:(t + 1) * 256],
                    uocS[:, m * 256:(m + 1) * 256])
                tlast = p.op("vector", fn, DV if t == B - 1 else None)
            h_t3[m] = tlast
            p.wait("scalar", DV, h_t3[m])
            h_done.append(p.op("scalar", lambda E, m=m: E.activation(
                h32[:, m * TOK:(m + 1) * TOK], t3[:], AF.Tanh,
                bias=bo_s[:, m:m + 1]), EV))

        yb16 = [ent(nc.sbuf_tensor("yb16_0", [128, 2048], BF16))] * 2
        sq16 = [ent(nc.sbuf_tensor("sq16_0", [128, 2048], BF16))] * 2
        stat_done = []
        y_done = []
        for fc in range(FC):
            sb = stage[fc % 2]
            p.wait("sync", CAST, t_uo)
            if fc >= 2:
                p.wait("sync", DV, y_done[fc - 2])
            t_ld = p.dma("sync", sb[:], x1s[fc * 128:(fc + 1) * 128, :], LD)
            p.wait("vector", LD, t_ld)
            p.wait("vector", EV, h_done[fc])
            y_done.append(p.op("vector", lambda E, fc=fc, sb=sb: E.tensor_add(
                h32[:, fc * TOK:(fc + 1) * TOK], h32[:, fc * TOK:(fc + 1) * TOK], sb[:]), DV))
            if fc >= 1:
                p.wait("vector", MM, stat_done[fc - 1])
            t_cast = p.op("vector", lambda E, fc=fc: E.tensor_copy(
                yb16[fc % 2][:], h32[:, fc * TOK:(fc + 1) * TOK]), DV)
            p.wait("scalar", DV, t_cast)
            t_sq = p.op("scalar", lambda E, fc=fc: E.activation(
                sq16[fc % 2][:], h32[:, fc * TOK:(fc + 1) * TOK], AF.Square), EV)
            p.wait("tensor", DV, t_cast)
            p.wait("tensor", EV, t_sq)
            lastmm = None
            for n in range(4):
                p.op("tensor", _mm(P0[0:1, n * 512:(n + 1) * 512], ones_col[:],
                                   yb16[fc % 2][:, n * 512:(n + 1) * 512],
                                   start=(fc == 0), stop=(fc == FC - 1)))
                lastmm = p.op("tensor", _mm(P1[0:1, n * 512:(n + 1) * 512], ones_col[:],
                                            sq16[fc % 2][:, n * 512:(n + 1) * 512],
                                            start=(fc == 0), stop=(fc == FC - 1)), MM)
            stat_done.append(lastmm)

        mu_t = stage[0][0:1, 0:TOK]
        aux_t = t3[0:1, 0:TOK]
        rstd_t = stage[1][0:1, 0:TOK]
        tB2 = emit_ln_scalar(p, P0, P1, mu_t, aux_t, rstd_t, eps_t, onesf_row,
                             MM, EV, DV, stat_done[-1])
        p.wait("vector", MM, tB2)

        x2scr = [t3, uocS]
        st_t = []
        for fc in range(FC):
            if fc >= 2:
                p.wait("vector", STS, st_t[fc - 2])
            p.op("vector", lambda E, fc=fc: E.tensor_mul(
                x2scr[fc % 2][:], h32[:, fc * TOK:(fc + 1) * TOK], P0[:]))
            t_ap = p.op("vector", lambda E, fc=fc: E.tensor_add(
                x2scr[fc % 2][:], x2scr[fc % 2][:], P1[:]), DV)
            p.wait("sync", DV, t_ap)
            st_t.append(p.dma("sync", x2s[fc * 128:(fc + 1) * 128, :], x2scr[fc % 2][:], STS))
        p.wait("sync", STS, st_t[-1])

        p.finalize()
    return nc


def _cc(a):
    return np.ascontiguousarray(a, dtype=np.float32)


def _pp(vec):
    v = np.asarray(vec, np.float32)
    n = v.shape[0] // 128
    return _cc(v.reshape(n, 128).T)


def stage_phaseB(x1T_cores, inp):
    wsT = _cc(np.concatenate([inp["L0_ffn_wf"], inp["L0_ffn_wi"], inp["L0_ffn_wc"]], 0).T)
    usT = _cc(np.concatenate([inp["L0_ffn_uf"], inp["L0_ffn_ui"], inp["L0_ffn_uc"]], 0).T)
    woT = _cc(np.asarray(inp["L0_ffn_wo"]).T)
    uoT = _cc(np.asarray(inp["L0_ffn_uo"]).T)
    bsrow = _cc(np.concatenate([inp["L0_ffn_bf"], inp["L0_ffn_bi"], inp["L0_ffn_bc"]], 0)[None, :])
    bo_pp = _pp(inp["L0_ffn_bo"])
    inc_pp = _pp(inp["L0_ffn_in_c"])
    maps = []
    for r in range(NCORES):
        x1slice = np.concatenate([x1T_cores[t][:, r * 256:(r + 1) * 256] for t in range(B)], axis=1)
        maps.append(dict(x1s=_cc(x1slice), wsT=wsT, usT=usT, woT=woT, uoT=uoT,
                         bsrow=bsrow, bo_pp=bo_pp, inc_pp=inc_pp))
    return maps


def unstage_phaseB(results):
    out = []
    for t in range(B):
        cols = [np.asarray(results[r]["x2s"], np.float32)[:, t * 256:(t + 1) * 256]
                for r in range(NCORES)]
        out.append(np.concatenate(cols, axis=1))
    return out




PHASEA_STOP = ""


def build_phaseA():
    nc = bass.Bass("TRN2", target_bir_lowering=False, detect_race_conditions=RACECHK)
    xT = nc.dram_tensor("xT", [H, TOK], F32, kind="ExternalInput").ap()
    wqkvT = nc.dram_tensor("wqkvT", [H, 3 * H], F32, kind="ExternalInput").ap()
    woutT = nc.dram_tensor("woutT", [H, H], F32, kind="ExternalInput").ap()
    g_pp = nc.dram_tensor("g_pp", [128, FC], F32, kind="ExternalInput").ap()
    be_pp = nc.dram_tensor("be_pp", [128, FC], F32, kind="ExternalInput").ap()
    bq_pp = nc.dram_tensor("bq_pp", [128, 3 * FC], F32, kind="ExternalInput").ap()
    bo_pp = nc.dram_tensor("bo_pp", [128, FC], F32, kind="ExternalInput").ap()
    x1T = nc.dram_tensor("x1T", [H, TOK], F32, kind="ExternalOutput").ap()

    with contextlib.ExitStack() as stack:
        ent = stack.enter_context
        p = Prog(nc, stack)
        ones_col = ent(nc.sbuf_tensor("ones_col", [128, 1], BF16))
        ones_row = ent(nc.sbuf_tensor("ones_row", [1, 512], BF16))
        onesf_row = ent(nc.sbuf_tensor("onesf_row", [1, 128], F32))
        ident = ent(nc.sbuf_tensor("ident", [128, 128], BF16))
        eps_t = ent(nc.sbuf_tensor("eps_t", [1, 1], F32))
        g_s = ent(nc.sbuf_tensor("g_s", [128, FC], F32))
        be_s = ent(nc.sbuf_tensor("be_s", [128, FC], F32))
        bq_s = ent(nc.sbuf_tensor("bq_s", [128, 3 * FC], F32))
        bo_s = ent(nc.sbuf_tensor("bo_s", [128, FC], F32))
        stage = [ent(nc.sbuf_tensor(f"stageA{i}", [128, 2048], F32)) for i in range(2)]
        sq16 = ent(nc.sbuf_tensor("sq16A", [128, 2048], BF16))
        yb16 = ent(nc.sbuf_tensor("yb16A", [128, 2048], BF16))
        CTX = ent(nc.sbuf_tensor("CTXA", [128, 8 * TOK], BF16))

        LD = Sem(stack, nc, "LD")
        CAST = Sem(stack, nc, "CAST")
        MM = Sem(stack, nc, "MM")
        EV = Sem(stack, nc, "EV")
        DV = Sem(stack, nc, "DV")
        STS = Sem(stack, nc, "STS")
        pipe = StagePipe(p, stage, LD, CAST)

        t_small = 0
        for (dst, src) in ((g_s[:], g_pp[:, :]), (be_s[:], be_pp[:, :]),
                           (bq_s[:], bq_pp[:, :]), (bo_s[:], bo_pp[:, :])):
            t_small = p.dma("sync", dst, src, LD)
        p.wait("vector", LD, t_small)
        p.op("vector", lambda E: E.memset(ones_col[:], 1.0))
        p.op("vector", lambda E: E.memset(ones_row[:], 1.0))
        p.op("vector", lambda E: E.memset(onesf_row[:], 1.0))
        t_eps = p.op("vector", lambda E: E.memset(eps_t[:], float(EPS)), DV)
        t_idm = p.op("gpsimd", lambda E: E.memset(ident[:], 0.0), CAST)
        p.wait("gpsimd", CAST, t_idm)
        t_id = p.op("gpsimd", lambda E: E.affine_select(
            out=ident[:], in_=ident[:], compare_op=ALU.not_equal, fill=1.0,
            base=0, pattern=[[-1, 128]], channel_multiplier=1), CAST)

        mu_t = stage[0][0:1, 0:TOK]
        rstd_t = stage[1][0:1, 0:TOK]

        xnb = ent(nc.sbuf_tensor("xnbA", [128, FC * TOK], BF16))
        stLN = contextlib.ExitStack()
        xb = stLN.enter_context(nc.sbuf_tensor("xbA", [128, FC * TOK], BF16))
        aux1 = stLN.enter_context(nc.sbuf_tensor("aux1", [1, TOK], F32))
        P0s = contextlib.ExitStack()
        P0 = P0s.enter_context(nc.psum_tensor("P0A", [128, 2048], F32))
        P1 = P0s.enter_context(nc.psum_tensor("P1A", [128, 2048], F32))
        stat_last = None
        sq_used = None
        for fc in range(FC):
            sb = stage[fc % 2]
            if fc >= 2:
                p.wait("sync", CAST, cast_ts[fc - 2])
                p.wait("sync", EV, sq_ts[fc - 2])
            t_ld = p.dma("sync", sb[:], xT[fc * 128:(fc + 1) * 128, :], LD)
            p.wait("vector", LD, t_ld)
            if fc == 0:
                cast_ts, sq_ts = [], []
            cast_ts.append(p.op("vector", lambda E, fc=fc, sb=sb: E.tensor_copy(
                xb[:, fc * TOK:(fc + 1) * TOK], sb[:]), CAST))
            p.wait("scalar", LD, t_ld)
            if fc >= 1:
                p.wait("scalar", MM, stat_last)
            sq_ts.append(p.op("scalar", lambda E, sb=sb: E.activation(
                sq16[:], sb[:], AF.Square), EV))
            p.wait("tensor", CAST, cast_ts[-1])
            p.wait("tensor", EV, sq_ts[-1])
            for n in range(4):
                p.op("tensor", _mm(P0[0:1, n * 512:(n + 1) * 512], ones_col[:],
                                   xb[:, fc * TOK + n * 512: fc * TOK + (n + 1) * 512],
                                   start=(fc == 0), stop=(fc == FC - 1)))
                stat_last = p.op("tensor", _mm(
                    P1[0:1, n * 512:(n + 1) * 512], ones_col[:],
                    sq16[:, n * 512:(n + 1) * 512],
                    start=(fc == 0), stop=(fc == FC - 1)), MM)
        tB2 = emit_ln_scalar(p, P0, P1, mu_t, aux1[0:1, :], rstd_t, eps_t, onesf_row,
                             MM, EV, DV, stat_last)
        p.wait("vector", MM, tB2)
        xn_ts = []
        for fc in range(FC):
            p.op("vector", lambda E, fc=fc, P0=P0: E.tensor_mul(
                xnb[:, fc * TOK:(fc + 1) * TOK], xb[:, fc * TOK:(fc + 1) * TOK], P0[:]))
            p.op("vector", lambda E, fc=fc, P1=P1: E.tensor_add(
                xnb[:, fc * TOK:(fc + 1) * TOK], xnb[:, fc * TOK:(fc + 1) * TOK], P1[:]))
            xn_ts.append(p.op("vector", lambda E, fc=fc: E.tensor_scalar(
                xnb[:, fc * TOK:(fc + 1) * TOK], xnb[:, fc * TOK:(fc + 1) * TOK],
                g_s[:, fc:fc + 1], be_s[:, fc:fc + 1], ALU.mult, ALU.add), DV))
        t_xn = xn_ts[-1]
        if PHASEA_STOP == "ln1":
            stLN.close()
            P0s.close()
            xs2 = [ent(nc.sbuf_tensor(f"xs2_{i}", [128, 2048], F32)) for i in range(2)]
            stq = []
            for fc in range(FC):
                if fc >= 2:
                    p.wait("vector", STS, stq[fc - 2])
                t_cp = p.op("vector", lambda E, fc=fc: E.tensor_copy(
                    xs2[fc % 2][:], xnb[:, fc * TOK:(fc + 1) * TOK]), DV)
                p.wait("sync", DV, t_cp)
                stq.append(p.dma("sync", x1T[fc * 128:(fc + 1) * 128, :], xs2[fc % 2][:], STS))
            p.wait("sync", STS, stq[-1])
            p.finalize()
            return nc
        stLN.close()
        P0s.close()

        stQ = contextlib.ExitStack()
        wqb = stQ.enter_context(nc.sbuf_tensor("wqbA", [128, FC * 3 * H], BF16))
        pq = stQ.enter_context(nc.sbuf_tensor("pqA", [128, 3 * TOK], BF16))
        Vtm = stQ.enter_context(nc.sbuf_tensor("VtmA", [128, 16 * 130], BF16))
        PtA = stQ.enter_context(nc.sbuf_tensor("PtA", [128, 1024], BF16))
        PtB = stQ.enter_context(nc.sbuf_tensor("PtB", [128, 1024], BF16))
        ctb = stQ.enter_context(nc.sbuf_tensor("ctbA", [128, 512], BF16))
        negm = stQ.enter_context(nc.sbuf_tensor("negmA", [128, 1], F32))
        rinv = stQ.enter_context(nc.sbuf_tensor("rinvA", [1, 512], F32))
        nrm = stQ.enter_context(nc.sbuf_tensor("nrmA", [1, 8], F32))

        p.wait("sync", CAST, cast_ts[-1])
        p.wait("sync", EV, sq_ts[-1])
        p.wait("sync", DV, t_xn)
        t_wq = None
        for k in range(FC):
            for c0 in (0, 2048):
                w = 1024 if c0 == 2048 else 2048
                t_wq = pipe.load_cast(wqkvT[k * 128:(k + 1) * 128, c0:c0 + w],
                                      wqb[:, k * 3 * H + c0: k * 3 * H + c0 + w], w, "gpsimd")
        if PHASEA_STOP == "wq":
            xs2 = [stQ.enter_context(nc.sbuf_tensor(f"xw2_{i}", [128, 2048], F32))
                   for i in range(2)]
            stq = []
            p.wait("vector", CAST, t_wq)
            for k in range(4):
                if k >= 2:
                    p.wait("vector", STS, stq[k - 2])
                t_cp = p.op("vector", lambda E, k=k: E.tensor_copy(
                    xs2[k % 2][:], wqb[:, k * 3 * H: k * 3 * H + 2048]), DV)
                p.wait("sync", DV, t_cp)
                stq.append(p.dma("sync", x1T[k * 128:(k + 1) * 128, :], xs2[k % 2][:], STS))
            p.wait("sync", STS, stq[-1])
            stQ.close()
            p.finalize()
            return nc
        p.wait("tensor", CAST, t_wq)
        p.wait("tensor", DV, t_xn)
        p.wait("tensor", CAST, t_id)

        t_vm = p.op("vector", lambda E: E.memset(Vtm[:], 1.0), DV)
        p.wait("tensor", DV, t_vm)

        ev_q = None
        last_rb = [0]
        for hp in range(8):
            stPQ = contextlib.ExitStack()
            Pq = [stPQ.enter_context(nc.psum_tensor(f"PqA{hp}_{i}", [128, 2048], F32))
                  for i in range(2)]
            ev3 = []
            for j, mo in enumerate((hp, 8 + hp, 16 + hp)):
                ps = Pq[(j + 1) % 2] if QKV_SWAP else Pq[j % 2]
                if j >= 2:
                    p.wait("tensor", DV, ev3[j - 2])
                lastq = None
                order = ([(k, n) for n in range(4) for k in range(FC)] if QKV_NMAJOR
                         else [(k, n) for k in range(FC) for n in range(4)])
                for (k, n) in order:
                    lastq = p.op("tensor", _mm(
                        ps[:, n * 512:(n + 1) * 512],
                        wqb[:, k * 3 * H + mo * 128: k * 3 * H + mo * 128 + 128],
                        xnb[:, k * TOK + n * 512: k * TOK + n * 512 + 512],
                        start=(k == 0), stop=(k == FC - 1)), MM)
                p.wait("vector", MM, lastq)
                ev3.append(p.op("vector", lambda E, ps=ps, j=j, mo=mo: E.tensor_scalar_add(
                    pq[:, j * TOK:(j + 1) * TOK], ps[:], bq_s[:, mo:mo + 1]), DV))
            if PHASEA_STOP == "qkv2" and hp == 0:
                for rep in range(2):
                    ps = Pq[rep % 2]
                    lastq = None
                    for k in range(FC):
                        for n in range(4):
                            lastq = p.op("tensor", _mm(
                                ps[:, n * 512:(n + 1) * 512],
                                wqb[:, k * 3 * H: k * 3 * H + 128],
                                xnb[:, k * TOK + n * 512: k * TOK + n * 512 + 512],
                                start=(k == 0), stop=(k == FC - 1)), MM)
                    p.wait("vector", MM, lastq)
                    ev3.append(p.op("vector", lambda E, ps=ps, rep=rep: E.tensor_scalar_add(
                        pq[:, rep * TOK:(rep + 1) * TOK], ps[:], bq_s[:, 0:1]), DV))
                ev_q = ev3[-1]
            ev_q = ev3[-1]
            stPQ.close()
            if PHASEA_STOP in ("qkv", "qkv2") and hp == 0:
                xs2 = [stQ.enter_context(nc.sbuf_tensor(f"xq2_{i}", [128, 2048], F32))
                       for i in range(2)]
                stq = []
                for j in range(3):
                    p.wait("vector", DV, ev_q)
                    if j >= 2:
                        p.wait("vector", STS, stq[j - 2])
                    t_cp = p.op("vector", lambda E, j=j: E.tensor_copy(
                        xs2[j % 2][:], pq[:, j * TOK:(j + 1) * TOK]), DV)
                    p.wait("sync", DV, t_cp)
                    if j >= 2:
                        p.wait("sync", STS, stq[j - 2])
                    stq.append(p.dma("sync", x1T[j * 128:(j + 1) * 128, :], xs2[j % 2][:], STS))
                p.wait("sync", STS, stq[-1])
                stQ.close()
                p.finalize()
                return nc

            stV = contextlib.ExitStack()
            ptran = stV.enter_context(nc.psum_tensor(f"ptrA{hp}", [128, 1024], BF16))
            p.wait("tensor", DV, ev_q)
            tr_ts = []
            for half in range(2):
                if half == 1:
                    p.wait("tensor", DV, tr_ts[-1])
                lastt = None
                for j8 in range(8):
                    jb = half * 8 + j8
                    lastt = p.op("tensor", lambda E, jb=jb, j8=j8, ptran=ptran: E.transpose(
                        ptran[:, j8 * 128:(j8 + 1) * 128],
                        pq[:, 2 * TOK + jb * 128: 2 * TOK + (jb + 1) * 128],
                        ident[:]), MM)
                p.wait("vector", MM, lastt)
                src = ptran[:].rearrange("p (j d) -> p j d", d=128)
                tr_ts.append(p.op("vector", lambda E, half=half, src=src: E.tensor_copy(
                    Vtm[:, half * 8 * 130:(half + 1) * 8 * 130].rearrange(
                        "p (j d) -> p j d", d=130)[:, :, 0:64],
                    src[:, :, 0:64]), DV))
                tr_ts.append(p.op("vector", lambda E, half=half, src=src: E.tensor_copy(
                    Vtm[:, half * 8 * 130:(half + 1) * 8 * 130].rearrange(
                        "p (j d) -> p j d", d=130)[:, :, 65:129],
                    src[:, :, 64:128]), DV))
            t_vt = tr_ts[-1]
            stV.close()

            stS = contextlib.ExitStack()
            sA = stS.enter_context(nc.psum_tensor(f"sA{hp}", [128, 1024], F32))
            sB = stS.enter_context(nc.psum_tensor(f"sB{hp}", [128, 1024], F32))
            pA = stS.enter_context(nc.psum_tensor(f"pA{hp}", [128, 512], F32))
            pB = stS.enter_context(nc.psum_tensor(f"pB{hp}", [128, 512], F32))

            p.wait("tensor", DV, t_vt)
            p.wait("scalar", DV, ev3[0])
            t_sqq = p.op("scalar", lambda E: E.activation(
                sq16[:], pq[:, 0:TOK], AF.Square), EV)
            p.wait("tensor", EV, t_sqq)
            lastn = None
            for n in range(4):
                lastn = p.op("tensor", _mm(
                    sA[0:1, (n % 2) * 512:(n % 2) * 512 + 512] if n < 2
                    else sB[0:1, (n - 2) * 512:(n - 2) * 512 + 512],
                    ones_col[:], sq16[:, n * 512:(n + 1) * 512],
                    start=True, stop=True), MM)
            p.wait("vector", MM, lastn)
            p.op("vector", lambda E, sA=sA: E.reduce_max(nrm[0:1, 0:1], sA[0:1, 0:1024], axis=AX.X))
            t_nq = p.op("vector", lambda E, sB=sB: E.reduce_max(nrm[0:1, 1:2], sB[0:1, 0:1024], axis=AX.X), DV)
            p.wait("scalar", DV, ev3[1])
            t_sqk = p.op("scalar", lambda E: E.activation(
                sq16[:], pq[:, TOK:2 * TOK], AF.Square), EV)
            p.wait("tensor", EV, t_sqk)
            p.wait("tensor", DV, t_nq)
            lastn = None
            for n in range(4):
                lastn = p.op("tensor", _mm(
                    sA[0:1, (n % 2) * 512:(n % 2) * 512 + 512] if n < 2
                    else sB[0:1, (n - 2) * 512:(n - 2) * 512 + 512],
                    ones_col[:], sq16[:, n * 512:(n + 1) * 512],
                    start=True, stop=True), MM)
            p.wait("vector", MM, lastn)
            p.op("vector", lambda E, sA=sA: E.reduce_max(nrm[0:1, 2:3], sA[0:1, 0:1024], axis=AX.X))
            p.op("vector", lambda E, sB=sB: E.reduce_max(nrm[0:1, 3:4], sB[0:1, 0:1024], axis=AX.X))
            p.op("vector", lambda E: E.tensor_max(nrm[0:1, 0:1], nrm[0:1, 0:1], nrm[0:1, 1:2]))
            p.op("vector", lambda E: E.tensor_max(nrm[0:1, 2:3], nrm[0:1, 2:3], nrm[0:1, 3:4]))
            t_nn = p.op("vector", lambda E: E.tensor_mul(nrm[0:1, 4:5], nrm[0:1, 0:1], nrm[0:1, 2:3]), DV)
            p.wait("scalar", DV, t_nn)
            t_ln5 = p.op("scalar", lambda E: E.activation(nrm[0:1, 5:6], nrm[0:1, 4:5], AF.Ln), EV)
            p.wait("scalar", EV, t_ln5)
            t_sr = p.op("scalar", lambda E: E.activation(nrm[0:1, 5:6], nrm[0:1, 5:6], AF.Exp, scale=0.5), EV)
            p.wait("vector", EV, t_sr)
            t_ng = p.op("vector", lambda E: E.tensor_scalar_mul(nrm[0:1, 6:7], nrm[0:1, 5:6], -0.125), DV)
            p.wait("tensor", DV, t_ng)
            t_bc = p.op("tensor", _mm(pA[:, 0:1], onesf_row[:], nrm[0:1, 6:7],
                                      start=True, stop=True), MM)
            p.wait("vector", MM, t_bc)
            t_negm = p.op("vector", lambda E, pA=pA: E.tensor_copy(negm[:], pA[:, 0:1]), DV)
            p.wait("scalar", DV, t_negm)
            p.wait("tensor", DV, t_negm)

            for qc in range(4):
                ev_s = {}
                ctx_last = None
                for jbp in range(8):
                    if jbp >= 1:
                        p.wait("tensor", EV, ev_s[jbp - 1])
                    lastS = None
                    for jh in range(2):
                        jb = jbp * 2 + jh
                        lastS = p.op("tensor", _mm(
                            sA[:, jh * 512:(jh + 1) * 512],
                            pq[0:64, TOK + jb * 128: TOK + jb * 128 + 128],
                            pq[0:64, qc * 512:(qc + 1) * 512],
                            start=True, stop=True), MM)
                        lastS = p.op("tensor", lambda E, jb=jb, jh=jh, qc=qc, sB=sB: E.matmul(
                            sB[:, jh * 512:(jh + 1) * 512],
                            pq[64:128, TOK + jb * 128: TOK + jb * 128 + 128],
                            pq[64:128, qc * 512:(qc + 1) * 512],
                            start=True, stop=True, tile_position=(64, 0)), MM)
                    p.wait("scalar", MM, lastS)
                    p.op("scalar", lambda E, sA=sA: E.activation(
                        PtA[:], sA[:], AF.Exp, bias=negm[:], scale=0.125))
                    ev_s[jbp] = p.op("scalar", lambda E, sB=sB: E.activation(
                        PtB[:], sB[:], AF.Exp, bias=negm[:], scale=0.125), EV)
                    p.wait("tensor", EV, ev_s[jbp])
                    for jh in range(2):
                        jb = jbp * 2 + jh
                        p.op("tensor", _mm(
                            pA[0:65, :], Vtm[:, jb * 130: jb * 130 + 65],
                            PtA[:, jh * 512:(jh + 1) * 512],
                            start=(jb == 0), stop=(jb == 15)), MM)
                        ctx_last = p.op("tensor", _mm(
                            pB[0:65, :], Vtm[:, jb * 130 + 65: jb * 130 + 130],
                            PtB[:, jh * 512:(jh + 1) * 512],
                            start=(jb == 0), stop=(jb == 15)), MM)
                for head, (pp, r0) in enumerate(((pA, 0), (pB, 64))):
                    p.wait("scalar", MM, ctx_last)
                    p.wait("scalar", MM, last_rb[0])
                    t_lnp = p.op("scalar", lambda E, pp=pp: E.activation(rinv[:], pp[64:65, :], AF.Ln), EV)
                    p.wait("scalar", EV, t_lnp)
                    t_ri = p.op("scalar", lambda E: E.activation(rinv[:], rinv[:], AF.Exp, scale=-1.0), EV)
                    p.wait("tensor", EV, t_ri)
                    t_rb = p.op("tensor", _mm(sA[0:64, 0:512], onesf_row[0:1, 0:64], rinv[:],
                                              start=True, stop=True), MM)
                    last_rb[0] = t_rb
                    p.wait("vector", MM, t_rb)
                    p.op("vector", lambda E, pp=pp: E.tensor_copy(ctb[0:64, :], pp[0:64, :]))
                    t_cx = p.op("vector", lambda E, hp=hp, qc=qc, r0=r0, sA=sA: E.tensor_mul(
                        CTX[r0:r0 + 64, hp * TOK + qc * 512: hp * TOK + (qc + 1) * 512],
                        ctb[0:64, :], sA[0:64, 0:512]), DV)
                    p.wait("tensor", DV, t_cx)
            stS.close()
            if PHASEA_STOP == "att1" and hp == 0:
                break
        if PHASEA_STOP in ("att", "att1"):
            stQ.close()
            xs2 = [ent(nc.sbuf_tensor(f"xa2_{i}", [128, 2048], F32)) for i in range(2)]
            stq = []
            nch = 8 if PHASEA_STOP == "att" else 1
            for fc in range(nch):
                if fc >= 2:
                    p.wait("vector", STS, stq[fc - 2])
                t_cp = p.op("vector", lambda E, fc=fc: E.tensor_copy(
                    xs2[fc % 2][:], CTX[:, fc * TOK:(fc + 1) * TOK]), DV)
                p.wait("sync", DV, t_cp)
                stq.append(p.dma("sync", x1T[fc * 128:(fc + 1) * 128, :], xs2[fc % 2][:], STS))
            p.wait("sync", STS, stq[-1])
            p.finalize()
            return nc
        stQ.close()

        wvb = ent(nc.sbuf_tensor("wvbA", [128, FC * H], BF16))
        x1p = ent(nc.sbuf_tensor("x1pA", [128, FC * TOK], F32))
        P0 = ent(nc.psum_tensor("P0A2", [128, 2048], F32))
        P1 = ent(nc.psum_tensor("P1A2", [128, 2048], F32))
        p.wait("sync", MM, MM.n)
        t_wv = None
        for k in range(FC):
            t_wv = pipe.load_cast(woutT[k * 128:(k + 1) * 128, :],
                                  wvb[:, k * H: (k + 1) * H], H, "gpsimd")
        p.wait("tensor", CAST, t_wv)
        PPo = [P0, P1]
        res_ts = []
        for mo in range(8):
            ps = PPo[mo % 2]
            if mo >= 2:
                p.wait("tensor", DV, res_ts[mo - 2])
            lasto = None
            for k in range(FC):
                for n in range(4):
                    lasto = p.op("tensor", _mm(
                        ps[:, n * 512:(n + 1) * 512],
                        wvb[:, k * H + mo * 128: k * H + mo * 128 + 128],
                        CTX[:, k * TOK + n * 512: k * TOK + n * 512 + 512],
                        start=(k == 0), stop=(k == FC - 1)), MM)
            sb = stage[mo % 2]
            if mo == 0:
                p.wait("sync", CAST, t_wv)
            if mo >= 2:
                p.wait("sync", DV, res_ts[mo - 2])
            t_ld = p.dma("sync", sb[:], xT[mo * 128:(mo + 1) * 128, :], LD)
            p.wait("vector", LD, t_ld)
            p.wait("vector", MM, lasto)
            p.op("vector", lambda E, mo=mo, ps=ps, sb=sb: E.scalar_tensor_tensor(
                x1p[:, mo * TOK:(mo + 1) * TOK], sb[:], 2.0, ps[:], ALU.mult, ALU.add))
            res_ts.append(p.op("vector", lambda E, mo=mo: E.tensor_scalar_add(
                x1p[:, mo * TOK:(mo + 1) * TOK], x1p[:, mo * TOK:(mo + 1) * TOK],
                bo_s[:, mo:mo + 1]), DV))

        stat_last = None
        st_sq = None
        for fc in range(FC):
            p.wait("vector", DV, res_ts[fc])
            if fc >= 1:
                p.wait("vector", MM, stat_last)
            t_yb = p.op("vector", lambda E, fc=fc: E.tensor_copy(
                yb16[:], x1p[:, fc * TOK:(fc + 1) * TOK]), DV)
            p.wait("scalar", DV, t_yb)
            if fc >= 1:
                p.wait("scalar", MM, stat_last)
            st_sq = p.op("scalar", lambda E, fc=fc: E.activation(
                sq16[:], x1p[:, fc * TOK:(fc + 1) * TOK], AF.Square), EV)
            p.wait("tensor", DV, t_yb)
            p.wait("tensor", EV, st_sq)
            for n in range(4):
                p.op("tensor", _mm(P0[0:1, n * 512:(n + 1) * 512], ones_col[:],
                                   yb16[:, n * 512:(n + 1) * 512],
                                   start=(fc == 0), stop=(fc == FC - 1)))
                stat_last = p.op("tensor", _mm(
                    P1[0:1, n * 512:(n + 1) * 512], ones_col[:],
                    sq16[:, n * 512:(n + 1) * 512],
                    start=(fc == 0), stop=(fc == FC - 1)), MM)
        x2scr = [ent(nc.sbuf_tensor(f"x1scr{i}", [128, 2048], F32)) for i in range(2)]
        aux_t = x2scr[0][0:1, 0:TOK]
        tB2 = emit_ln_scalar(p, P0, P1, mu_t, aux_t, rstd_t, eps_t, onesf_row,
                             MM, EV, DV, stat_last)
        p.wait("vector", MM, tB2)
        st_t = []
        for fc in range(FC):
            if fc >= 2:
                p.wait("vector", STS, st_t[fc - 2])
            p.op("vector", lambda E, fc=fc: E.tensor_mul(
                x2scr[fc % 2][:], x1p[:, fc * TOK:(fc + 1) * TOK], P0[:]))
            t_ap = p.op("vector", lambda E, fc=fc: E.tensor_add(
                x2scr[fc % 2][:], x2scr[fc % 2][:], P1[:]), DV)
            p.wait("sync", DV, t_ap)
            st_t.append(p.dma("sync", x1T[fc * 128:(fc + 1) * 128, :], x2scr[fc % 2][:], STS))
        p.wait("sync", STS, st_t[-1])
        p.finalize()
    return nc


def stage_phaseA(inp):
    x = np.asarray(inp["L0_attn_hidden_states"], np.float32)
    wqkvT = _cc(np.asarray(inp["L0_attn_W_qkv"]).T)
    woutT = _cc(np.asarray(inp["L0_attn_W_out"]).T)
    g_pp = _pp(inp["L0_attn_ln_gamma"])
    be_pp = _pp(inp["L0_attn_ln_beta"])
    bq_pp = _pp(inp["L0_attn_b_qkv"])
    bo_pp = _pp(inp["L0_attn_b_out"])
    maps = []
    for b in range(NCORES):
        maps.append(dict(xT=_cc(x[b].T), wqkvT=wqkvT, woutT=woutT,
                         g_pp=g_pp, be_pp=be_pp, bq_pp=bq_pp, bo_pp=bo_pp))
    return maps




def build_phaseC():
    nc = bass.Bass("TRN2", target_bir_lowering=False, detect_race_conditions=RACECHK)
    x2T = nc.dram_tensor("x2T", [H, TOK], F32, kind="ExternalInput").ap()
    wqkvT1 = nc.dram_tensor("wqkvT1", [H, 3 * H], F32, kind="ExternalInput").ap()
    bq1_pp = nc.dram_tensor("bq1_pp", [128, 3 * FC], F32, kind="ExternalInput").ap()
    woT1 = nc.dram_tensor("woT1", [H, H], F32, kind="ExternalInput").ap()
    bo1_pp = nc.dram_tensor("bo1_pp", [128, FC], F32, kind="ExternalInput").ap()
    Wg = nc.dram_tensor("Wg", [H, 8], F32, kind="ExternalInput").ap()
    bg_row = nc.dram_tensor("bg_row", [1, 8], F32, kind="ExternalInput").ap()
    W1 = nc.dram_tensor("W1", [8, H, 4 * H], F32, kind="ExternalInput").ap()
    b1_pp = nc.dram_tensor("b1_pp", [128, 8 * 32], F32, kind="ExternalInput").ap()
    W2 = nc.dram_tensor("W2", [8, 4 * H, H], F32, kind="ExternalInput").ap()
    b2s = nc.dram_tensor("b2s", [8, H], F32, kind="ExternalInput").ap()
    outT = nc.dram_tensor("outT", [H, TOK], F32, kind="ExternalOutput").ap()
    x3_dram = nc.dram_tensor("x3_dram", [H, TOK], F32,
                             kind="ExternalOutput" if PHASEC_DBG else "Internal").ap()
    dbg_w8 = (nc.dram_tensor("dbg_w8", [8, TOK], F32, kind="ExternalOutput").ap()
              if PHASEC_DBG else None)

    SCALE = float(1.0 / np.sqrt(1024.0))

    with contextlib.ExitStack() as stack:
        ent = stack.enter_context
        p = Prog(nc, stack)
        ones_col = ent(nc.sbuf_tensor("ones_col", [128, 1], BF16))
        onesf_row = ent(nc.sbuf_tensor("onesf_row", [1, 512], F32))
        ident = ent(nc.sbuf_tensor("ident", [128, 128], BF16))
        identf = ent(nc.sbuf_tensor("identf", [128, 128], F32))
        eps_t = ent(nc.sbuf_tensor("eps_t", [1, 1], F32))
        bq_s = ent(nc.sbuf_tensor("bq_s", [128, 3 * FC], F32))
        bo_s = ent(nc.sbuf_tensor("bo_s", [128, FC], F32))
        b1_s = ent(nc.sbuf_tensor("b1_s", [128, 8 * 32], F32))
        stage = [ent(nc.sbuf_tensor(f"stageC{i}", [128, 1024], F32)) for i in range(2)]
        sq16 = ent(nc.sbuf_tensor("sq16C", [128, 2048], BF16))
        yb16 = ent(nc.sbuf_tensor("yb16C", [128, 1024], BF16))
        x2b = ent(nc.sbuf_tensor("x2bC", [128, FC * TOK], BF16))
        w8T = ent(nc.sbuf_tensor("w8T", [8, TOK], F32))
        w_row = ent(nc.sbuf_tensor("w_row", [1, TOK], F32))
        stBig = contextlib.ExitStack()
        CTX = stBig.enter_context(nc.sbuf_tensor("CTXC", [128, 8 * TOK], BF16))
        Wg_s = stBig.enter_context(nc.sbuf_tensor("Wg_s", [128, FC * 8], F32))
        bg_s = stBig.enter_context(nc.sbuf_tensor("bg_s", [1, 8], F32))
        gsb = stBig.enter_context(nc.sbuf_tensor("gsbC", [8, TOK], F32))
        gtok = stBig.enter_context(nc.sbuf_tensor("gtokC", [128, 128], F32))
        wtok = stBig.enter_context(nc.sbuf_tensor("wtokC", [128, 128], F32))
        gaux = stBig.enter_context(nc.sbuf_tensor("gauxC", [128, 8], F32))
        gax2 = stBig.enter_context(nc.sbuf_tensor("gax2C", [128, 8], F32))
        gcol = stBig.enter_context(nc.sbuf_tensor("gcolC", [128, 3], F32))

        LD = Sem(stack, nc, "LD")
        CAST = Sem(stack, nc, "CAST")
        MM = Sem(stack, nc, "MM")
        EV = Sem(stack, nc, "EV")
        DV = Sem(stack, nc, "DV")
        STS = Sem(stack, nc, "STS")
        pipe = StagePipe(p, stage, LD, CAST)

        t_small = 0
        for (dst, src) in ((bq_s[:], bq1_pp[:, :]), (bo_s[:], bo1_pp[:, :]),
                           (b1_s[:], b1_pp[:, :])):
            t_small = p.dma("sync", dst, src, LD)
        p.wait("vector", LD, t_small)
        p.op("vector", lambda E: E.memset(ones_col[:], 1.0))
        p.op("vector", lambda E: E.memset(onesf_row[:], 1.0))
        t_eps = p.op("vector", lambda E: E.memset(eps_t[:], float(EPS)), DV)
        t_idm = p.op("gpsimd", lambda E: E.memset(ident[:], 0.0), CAST)
        p.wait("gpsimd", CAST, t_idm)
        t_id = p.op("gpsimd", lambda E: E.affine_select(
            out=ident[:], in_=ident[:], compare_op=ALU.not_equal, fill=1.0,
            base=0, pattern=[[-1, 128]], channel_multiplier=1), CAST)
        t_idm2 = p.op("gpsimd", lambda E: E.memset(identf[:], 0.0), CAST)
        p.wait("gpsimd", CAST, t_idm2)
        t_idf = p.op("gpsimd", lambda E: E.affine_select(
            out=identf[:], in_=identf[:], compare_op=ALU.not_equal, fill=1.0,
            base=0, pattern=[[-1, 128]], channel_multiplier=1), CAST)

        mu_t = stage[0][0:1, 0:1024]
        rstd_t = stage[1][0:1, 0:1024]

        for fc in range(FC):
            for g2 in range(2):
                t_x2b = pipe.load_cast(
                    x2T[fc * 128:(fc + 1) * 128, g2 * 1024:(g2 + 1) * 1024],
                    x2b[:, fc * TOK + g2 * 1024: fc * TOK + (g2 + 1) * 1024], 1024, "vector")

        stQ = contextlib.ExitStack()
        wqb = stQ.enter_context(nc.sbuf_tensor("wqbC", [128, FC * 3 * H], BF16))
        pq = stQ.enter_context(nc.sbuf_tensor("pqC", [128, 3 * TOK], BF16))
        Vtm = stQ.enter_context(nc.sbuf_tensor("VtmC", [128, 16 * 130], BF16))
        PtA = stQ.enter_context(nc.sbuf_tensor("PtC_A", [128, 1024], BF16))
        PtB = stQ.enter_context(nc.sbuf_tensor("PtC_B", [128, 1024], BF16))
        ctb = stQ.enter_context(nc.sbuf_tensor("ctbC", [128, 512], BF16))
        negm = stQ.enter_context(nc.sbuf_tensor("negmC", [128, 1], F32))
        rinv = stQ.enter_context(nc.sbuf_tensor("rinvC", [1, 512], F32))
        nrm = stQ.enter_context(nc.sbuf_tensor("nrmC", [1, 8], F32))

        t_wq = None
        for k in range(FC):
            for c0 in range(0, 3 * H, 1024):
                t_wq = pipe.load_cast(wqkvT1[k * 128:(k + 1) * 128, c0:c0 + 1024],
                                      wqb[:, k * 3 * H + c0: k * 3 * H + c0 + 1024],
                                      1024, "gpsimd")
        p.wait("tensor", CAST, t_wq)
        p.wait("tensor", CAST, t_idf)
        t_vm = p.op("vector", lambda E: E.memset(Vtm[:], 1.0), DV)
        p.wait("tensor", DV, t_vm)

        last_rb = [0]
        for hp in range(8):
            stPQ = contextlib.ExitStack()
            Pq = [stPQ.enter_context(nc.psum_tensor(f"PqC{hp}_{i}", [128, 2048], F32))
                  for i in range(2)]
            ev3 = []
            for j, mo in enumerate((hp, 8 + hp, 16 + hp)):
                ps = Pq[j % 2]
                if j >= 2:
                    p.wait("tensor", DV, ev3[j - 2])
                lastq = None
                for k in range(FC):
                    for n in range(4):
                        lastq = p.op("tensor", _mm(
                            ps[:, n * 512:(n + 1) * 512],
                            wqb[:, k * 3 * H + mo * 128: k * 3 * H + mo * 128 + 128],
                            x2b[:, k * TOK + n * 512: k * TOK + n * 512 + 512],
                            start=(k == 0), stop=(k == FC - 1)), MM)
                p.wait("vector", MM, lastq)
                ev3.append(p.op("vector", lambda E, ps=ps, j=j, mo=mo: E.tensor_scalar_add(
                    pq[:, j * TOK:(j + 1) * TOK], ps[:], bq_s[:, mo:mo + 1]), DV))
            ev_q = ev3[-1]
            stPQ.close()

            stV = contextlib.ExitStack()
            ptran = stV.enter_context(nc.psum_tensor(f"ptrC{hp}", [128, 1024], BF16))
            p.wait("tensor", DV, ev_q)
            tr_ts = []
            for half in range(2):
                if half == 1:
                    p.wait("tensor", DV, tr_ts[-1])
                lastt = None
                for j8 in range(8):
                    jb = half * 8 + j8
                    lastt = p.op("tensor", lambda E, jb=jb, j8=j8, ptran=ptran: E.transpose(
                        ptran[:, j8 * 128:(j8 + 1) * 128],
                        pq[:, 2 * TOK + jb * 128: 2 * TOK + (jb + 1) * 128],
                        ident[:]), MM)
                p.wait("vector", MM, lastt)
                src = ptran[:].rearrange("p (j d) -> p j d", d=128)
                tr_ts.append(p.op("vector", lambda E, half=half, src=src: E.tensor_copy(
                    Vtm[:, half * 8 * 130:(half + 1) * 8 * 130].rearrange(
                        "p (j d) -> p j d", d=130)[:, :, 0:64],
                    src[:, :, 0:64]), DV))
                tr_ts.append(p.op("vector", lambda E, half=half, src=src: E.tensor_copy(
                    Vtm[:, half * 8 * 130:(half + 1) * 8 * 130].rearrange(
                        "p (j d) -> p j d", d=130)[:, :, 65:129],
                    src[:, :, 64:128]), DV))
            t_vt = tr_ts[-1]
            stV.close()

            stS = contextlib.ExitStack()
            sA = stS.enter_context(nc.psum_tensor(f"sC_A{hp}", [128, 1024], F32))
            sB = stS.enter_context(nc.psum_tensor(f"sC_B{hp}", [128, 1024], F32))
            pA = stS.enter_context(nc.psum_tensor(f"pC_A{hp}", [128, 512], F32))
            pB = stS.enter_context(nc.psum_tensor(f"pC_B{hp}", [128, 512], F32))

            p.wait("tensor", DV, t_vt)
            p.wait("scalar", DV, ev3[0])
            t_sqq = p.op("scalar", lambda E: E.activation(
                sq16[:], pq[:, 0:TOK], AF.Square), EV)
            p.wait("tensor", EV, t_sqq)
            lastn = None
            for n in range(4):
                lastn = p.op("tensor", _mm(
                    sA[0:1, (n % 2) * 512:(n % 2) * 512 + 512] if n < 2
                    else sB[0:1, (n - 2) * 512:(n - 2) * 512 + 512],
                    ones_col[:], sq16[:, n * 512:(n + 1) * 512],
                    start=True, stop=True), MM)
            p.wait("vector", MM, lastn)
            p.op("vector", lambda E, sA=sA: E.reduce_max(nrm[0:1, 0:1], sA[0:1, 0:1024], axis=AX.X))
            t_nq = p.op("vector", lambda E, sB=sB: E.reduce_max(nrm[0:1, 1:2], sB[0:1, 0:1024], axis=AX.X), DV)
            p.wait("scalar", DV, ev3[1])
            t_sqk = p.op("scalar", lambda E: E.activation(
                sq16[:], pq[:, TOK:2 * TOK], AF.Square), EV)
            p.wait("tensor", EV, t_sqk)
            p.wait("tensor", DV, t_nq)
            lastn = None
            for n in range(4):
                lastn = p.op("tensor", _mm(
                    sA[0:1, (n % 2) * 512:(n % 2) * 512 + 512] if n < 2
                    else sB[0:1, (n - 2) * 512:(n - 2) * 512 + 512],
                    ones_col[:], sq16[:, n * 512:(n + 1) * 512],
                    start=True, stop=True), MM)
            p.wait("vector", MM, lastn)
            p.op("vector", lambda E, sA=sA: E.reduce_max(nrm[0:1, 2:3], sA[0:1, 0:1024], axis=AX.X))
            p.op("vector", lambda E, sB=sB: E.reduce_max(nrm[0:1, 3:4], sB[0:1, 0:1024], axis=AX.X))
            p.op("vector", lambda E: E.tensor_max(nrm[0:1, 0:1], nrm[0:1, 0:1], nrm[0:1, 1:2]))
            p.op("vector", lambda E: E.tensor_max(nrm[0:1, 2:3], nrm[0:1, 2:3], nrm[0:1, 3:4]))
            t_nn = p.op("vector", lambda E: E.tensor_mul(nrm[0:1, 4:5], nrm[0:1, 0:1], nrm[0:1, 2:3]), DV)
            p.wait("scalar", DV, t_nn)
            t_ln5 = p.op("scalar", lambda E: E.activation(nrm[0:1, 5:6], nrm[0:1, 4:5], AF.Ln), EV)
            p.wait("scalar", EV, t_ln5)
            t_sr = p.op("scalar", lambda E: E.activation(nrm[0:1, 5:6], nrm[0:1, 5:6], AF.Exp, scale=0.5), EV)
            p.wait("vector", EV, t_sr)
            t_ng = p.op("vector", lambda E: E.tensor_scalar_mul(nrm[0:1, 6:7], nrm[0:1, 5:6], -SCALE), DV)
            p.wait("tensor", DV, t_ng)
            t_bc = p.op("tensor", _mm(pA[:, 0:1], onesf_row[0:1, 0:128], nrm[0:1, 6:7],
                                      start=True, stop=True), MM)
            p.wait("vector", MM, t_bc)
            t_negm = p.op("vector", lambda E, pA=pA: E.tensor_copy(negm[:], pA[:, 0:1]), DV)
            p.wait("scalar", DV, t_negm)
            p.wait("tensor", DV, t_negm)

            for qc in range(4):
                ev_s = {}
                ctx_last = None
                for jbp in range(8):
                    if jbp >= 1:
                        p.wait("tensor", EV, ev_s[jbp - 1])
                    lastS = None
                    for jh in range(2):
                        jb = jbp * 2 + jh
                        lastS = p.op("tensor", _mm(
                            sA[:, jh * 512:(jh + 1) * 512],
                            pq[0:64, jb * 128: jb * 128 + 128],
                            pq[0:64, TOK + qc * 512: TOK + (qc + 1) * 512],
                            start=True, stop=True), MM)
                        lastS = p.op("tensor", lambda E, jb=jb, jh=jh, qc=qc, sB=sB: E.matmul(
                            sB[:, jh * 512:(jh + 1) * 512],
                            pq[64:128, jb * 128: jb * 128 + 128],
                            pq[64:128, TOK + qc * 512: TOK + (qc + 1) * 512],
                            start=True, stop=True, tile_position=(64, 0)), MM)
                    p.wait("scalar", MM, lastS)
                    p.op("scalar", lambda E, sA=sA: E.activation(
                        PtA[:], sA[:], AF.Exp, bias=negm[:], scale=SCALE))
                    ev_s[jbp] = p.op("scalar", lambda E, sB=sB: E.activation(
                        PtB[:], sB[:], AF.Exp, bias=negm[:], scale=SCALE), EV)
                    p.wait("tensor", EV, ev_s[jbp])
                    for jh in range(2):
                        jb = jbp * 2 + jh
                        p.op("tensor", _mm(
                            pA[0:65, :], Vtm[:, jb * 130: jb * 130 + 65],
                            PtA[:, jh * 512:(jh + 1) * 512],
                            start=(jb == 0), stop=(jb == 15)), MM)
                        ctx_last = p.op("tensor", _mm(
                            pB[0:65, :], Vtm[:, jb * 130 + 65: jb * 130 + 130],
                            PtB[:, jh * 512:(jh + 1) * 512],
                            start=(jb == 0), stop=(jb == 15)), MM)
                for head, (pp, r0) in enumerate(((pA, 0), (pB, 64))):
                    p.wait("scalar", MM, ctx_last)
                    p.wait("scalar", MM, last_rb[0])
                    t_lnp = p.op("scalar", lambda E, pp=pp: E.activation(rinv[:], pp[64:65, :], AF.Ln), EV)
                    p.wait("scalar", EV, t_lnp)
                    t_ri = p.op("scalar", lambda E: E.activation(rinv[:], rinv[:], AF.Exp, scale=-1.0), EV)
                    p.wait("tensor", EV, t_ri)
                    t_rb = p.op("tensor", _mm(sA[0:64, 0:512], onesf_row[0:1, 0:64], rinv[:],
                                              start=True, stop=True), MM)
                    last_rb[0] = t_rb
                    p.wait("vector", MM, t_rb)
                    p.op("vector", lambda E, pp=pp: E.tensor_copy(ctb[0:64, :], pp[0:64, :]))
                    t_cx = p.op("vector", lambda E, hp=hp, qc=qc, r0=r0, sA=sA: E.tensor_mul(
                        CTX[r0:r0 + 64, hp * TOK + qc * 512: hp * TOK + (qc + 1) * 512],
                        ctb[0:64, :], sA[0:64, 0:512]), DV)
                    p.wait("tensor", DV, t_cx)
            stS.close()
        stQ.close()

        wvb = stBig.enter_context(nc.sbuf_tensor("wvbC", [128, FC * H], BF16))
        x3p = stBig.enter_context(nc.sbuf_tensor("x3pC", [128, FC * TOK], F32))
        stP2 = contextlib.ExitStack()
        P0 = stP2.enter_context(nc.psum_tensor("P0C", [128, 2048], F32))
        P1 = stP2.enter_context(nc.psum_tensor("P1C", [128, 2048], F32))
        p.wait("sync", MM, MM.n)
        t_wv = None
        for k in range(FC):
            t_wv = pipe.load_cast(woT1[k * 128:(k + 1) * 128, 0:1024],
                                  wvb[:, k * H: k * H + 1024], 1024, "gpsimd")
        p.wait("tensor", CAST, t_wv)
        PPo = [P0, P1]
        res_ts = []
        for mo in range(8):
            ps = PPo[mo % 2]
            if mo >= 2:
                p.wait("tensor", DV, res_ts[mo - 2])
            lasto = None
            for k in range(FC):
                for n in range(4):
                    lasto = p.op("tensor", _mm(
                        ps[:, n * 512:(n + 1) * 512],
                        wvb[:, k * H + mo * 128: k * H + mo * 128 + 128],
                        CTX[:, k * TOK + n * 512: k * TOK + n * 512 + 512],
                        start=(k == 0), stop=(k == FC - 1)), MM)
            for g2 in range(2):
                sb = stage[g2]
                if mo == 0 and g2 == 0:
                    p.wait("sync", CAST, t_wv)
                if mo >= 1 or g2 >= 1:
                    p.wait("sync", DV, res_prev)
                t_ld = p.dma("sync", sb[:], x2T[mo * 128:(mo + 1) * 128,
                                                g2 * 1024:(g2 + 1) * 1024], LD)
                p.wait("vector", LD, t_ld)
                p.wait("vector", MM, lasto)
                p.op("vector", lambda E, mo=mo, g2=g2, ps=ps, sb=sb: E.tensor_add(
                    x3p[:, mo * TOK + g2 * 1024: mo * TOK + (g2 + 1) * 1024],
                    sb[:], ps[:, g2 * 1024:(g2 + 1) * 1024]))
                res_prev = p.op("vector", lambda E, mo=mo, g2=g2: E.tensor_scalar_add(
                    x3p[:, mo * TOK + g2 * 1024: mo * TOK + (g2 + 1) * 1024],
                    x3p[:, mo * TOK + g2 * 1024: mo * TOK + (g2 + 1) * 1024],
                    bo_s[:, mo:mo + 1]), DV)
            res_ts.append(res_prev)

        x3b = x2b
        st_x3 = []
        for g2 in range(2):
            stat_last = None
            for fc in range(FC):
                p.wait("vector", DV, res_ts[fc])
                if stat_last is not None:
                    p.wait("vector", MM, stat_last)
                t_yb = p.op("vector", lambda E, fc=fc, g2=g2: E.tensor_copy(
                    yb16[:], x3p[:, fc * TOK + g2 * 1024: fc * TOK + (g2 + 1) * 1024]), DV)
                p.wait("scalar", DV, t_yb)
                if stat_last is not None:
                    p.wait("scalar", MM, stat_last)
                t_sq = p.op("scalar", lambda E, fc=fc, g2=g2: E.activation(
                    sq16[:, 0:1024], x3p[:, fc * TOK + g2 * 1024: fc * TOK + (g2 + 1) * 1024],
                    AF.Square), EV)
                p.wait("tensor", DV, t_yb)
                p.wait("tensor", EV, t_sq)
                for n in range(2):
                    p.op("tensor", _mm(P0[0:1, n * 512:(n + 1) * 512], ones_col[:],
                                       yb16[:, n * 512:(n + 1) * 512],
                                       start=(fc == 0), stop=(fc == FC - 1)))
                    stat_last = p.op("tensor", _mm(
                        P1[0:1, n * 512:(n + 1) * 512], ones_col[:],
                        sq16[:, n * 512:(n + 1) * 512],
                        start=(fc == 0), stop=(fc == FC - 1)), MM)
            tB2 = emit_ln_scalar2(p, P0, P1, mu_t, w_row[0:1, 0:1024], rstd_t, eps_t,
                                  onesf_row, MM, EV, DV, stat_last, width=1024)
            p.wait("vector", MM, tB2)
            for fc in range(FC):
                p.op("vector", lambda E, fc=fc, g2=g2, P0=P0: E.tensor_mul(
                    x3p[:, fc * TOK + g2 * 1024: fc * TOK + (g2 + 1) * 1024],
                    x3p[:, fc * TOK + g2 * 1024: fc * TOK + (g2 + 1) * 1024], P0[:, 0:1024]))
                t_ap = p.op("vector", lambda E, fc=fc, g2=g2, P1=P1: E.tensor_add(
                    x3p[:, fc * TOK + g2 * 1024: fc * TOK + (g2 + 1) * 1024],
                    x3p[:, fc * TOK + g2 * 1024: fc * TOK + (g2 + 1) * 1024], P1[:, 0:1024]), DV)
                t_cb = p.op("vector", lambda E, fc=fc, g2=g2: E.tensor_copy(
                    x3b[:, fc * TOK + g2 * 1024: fc * TOK + (g2 + 1) * 1024],
                    x3p[:, fc * TOK + g2 * 1024: fc * TOK + (g2 + 1) * 1024]), DV)
                p.wait("sync", DV, t_cb)
                st_x3.append(p.dma("sync", x3_dram[fc * 128:(fc + 1) * 128,
                                                   g2 * 1024:(g2 + 1) * 1024],
                                   x3p[:, fc * TOK + g2 * 1024: fc * TOK + (g2 + 1) * 1024], STS))

        t_ldg = 0
        for k in range(FC):
            t_ldg = p.dma("sync", Wg_s[:, k * 8:(k + 1) * 8], Wg[k * 128:(k + 1) * 128, :], LD)
        t_ldg = p.dma("sync", bg_s[:], bg_row[:, :], LD)
        p.wait("tensor", LD, t_ldg)
        p.wait("tensor", DV, t_ap)
        stP2.close()
        stP3 = contextlib.ExitStack()
        Pg = stP3.enter_context(nc.psum_tensor("PgC", [8, TOK], F32))
        lastg = None
        for n in range(4):
            for k in range(FC):
                p.op("tensor", _mm(Pg[0:8, n * 512:(n + 1) * 512],
                                   Wg_s[:, k * 8:(k + 1) * 8],
                                   x3p[:, k * TOK + n * 512: k * TOK + (n + 1) * 512],
                                   start=(k == 0), stop=False), MM)
            lastg = p.op("tensor", _mm(Pg[0:8, n * 512:(n + 1) * 512],
                                       bg_s[:], onesf_row[0:1, 0:512],
                                       start=False, stop=True), MM)
        p.wait("vector", MM, lastg)
        t_gs = p.op("vector", lambda E, Pg=Pg: E.tensor_copy(gsb[:], Pg[0:8, :]), DV)
        stP3.close()
        stP4 = contextlib.ExitStack()
        Pt8 = stP4.enter_context(nc.psum_tensor("Pt8C", [128, 128], F32))
        Pw8 = stP4.enter_context(nc.psum_tensor("Pw8C", [8, TOK], F32))
        p.wait("tensor", DV, t_gs)
        lastt = None
        for t in range(16):
            lastt = p.op("tensor", lambda E, t=t, Pt8=Pt8: E.transpose(
                Pt8[:, t * 8:(t + 1) * 8], gsb[0:8, t * 128:(t + 1) * 128],
                identf[0:8, 0:8]), MM)
        p.wait("vector", MM, lastt)
        t_gt = p.op("vector", lambda E, Pt8=Pt8: E.tensor_copy(gtok[:], Pt8[:]), DV)
        wt_ts = []

        def dvs(fn):
            t = p.op("vector", fn, DV)
            p.wait("vector", DV, t)
            return t

        for t in range(16):
            gt = gtok[:, t * 8:(t + 1) * 8]
            wt = wtok[:, t * 8:(t + 1) * 8]
            if t == 0:
                p.wait("vector", DV, t_gt)
            tmx = dvs(lambda E, gt=gt: E.reduce_max(gcol[:, 0:1], gt, axis=AX.X, negate=True))
            p.wait("scalar", DV, tmx)
            t_e = p.op("scalar", lambda E, gt=gt, wt=wt: E.activation(
                wt, gt, AF.Exp, bias=gcol[:, 0:1], accum_out=gcol[:, 1:2]), EV)
            p.wait("vector", EV, t_e)
            dvs(lambda E: E.reciprocal(gcol[:, 2:3], gcol[:, 1:2]))
            dvs(lambda E, wt=wt: E.tensor_scalar_mul(wt, wt, gcol[:, 2:3]))
            dvs(lambda E, wt=wt: E.reduce_max(gcol[:, 0:1], wt, axis=AX.X))
            dvs(lambda E, wt=wt: E.tensor_scalar(
                gaux[:], wt, gcol[:, 0:1], None, ALU.is_ge))
            dvs(lambda E, wt=wt: E.scalar_tensor_tensor(
                gax2[:], gaux[:], -1e9, wt, ALU.mult, ALU.add))
            dvs(lambda E: E.reduce_max(gcol[:, 0:1], gax2[:], axis=AX.X))
            dvs(lambda E, wt=wt: E.tensor_scalar(
                gaux[:], wt, gcol[:, 0:1], None, ALU.is_ge))
            dvs(lambda E, wt=wt: E.tensor_mul(wt, wt, gaux[:]))
            dvs(lambda E, wt=wt: E.reduce_sum(gcol[:, 0:1], wt, axis=AX.X))
            dvs(lambda E: E.tensor_scalar(
                gcol[:, 0:1], gcol[:, 0:1], 1e-8, None, ALU.add))
            dvs(lambda E: E.reciprocal(gcol[:, 2:3], gcol[:, 0:1]))
            wt_ts.append(dvs(lambda E, wt=wt: E.tensor_scalar_mul(
                wt, wt, gcol[:, 2:3])))
        p.wait("tensor", DV, wt_ts[-1])
        lastt = None
        for t in range(16):
            lastt = p.op("tensor", lambda E, t=t, Pw8=Pw8: E.transpose(
                Pw8[0:8, t * 128:(t + 1) * 128], wtok[:, t * 8:(t + 1) * 8], identf[:]), MM)
        p.wait("vector", MM, lastt)
        t_w8 = p.op("vector", lambda E, Pw8=Pw8: E.tensor_copy(w8T[:], Pw8[0:8, :]), DV)
        if PHASEC_DBG:
            p.wait("sync", DV, t_w8)
            p.dma("sync", dbg_w8[:, :], w8T[:], STS)
        stP4.close()
        p.wait("sync", MM, MM.n)
        p.wait("sync", STS, st_x3[-1])
        stBig.close()
        _phaseC_moe(nc, stack, p, locals())
    return nc


def emit_ln_scalar2(p, P0, P1, mu_t, aux_t, rstd_t, eps_t, onesf_row,
                    MM, EV, DV, stat_last, width=2048):
    p.wait("vector", MM, stat_last)
    p.op("vector", lambda E: E.tensor_scalar_mul(mu_t, P0[0:1, 0:width], 1.0 / H))
    t_var = p.op("vector", lambda E: E.tensor_scalar_mul(rstd_t, P1[0:1, 0:width], 1.0 / H), DV)
    p.wait("scalar", DV, t_var)
    t_mu2 = p.op("scalar", lambda E: E.activation(aux_t, mu_t, AF.Square), EV)
    p.wait("vector", EV, t_mu2)
    t_v2 = p.op("vector", lambda E: E.tensor_sub(rstd_t, rstd_t, aux_t), DV)
    p.wait("scalar", DV, t_v2)
    t_lnr = p.op("scalar", lambda E: E.activation(rstd_t, rstd_t, AF.Ln, bias=eps_t[:]), EV)
    p.wait("scalar", EV, t_lnr)
    t_rstd = p.op("scalar", lambda E: E.activation(rstd_t, rstd_t, AF.Exp, scale=-0.5), EV)
    p.wait("vector", EV, t_rstd)
    t_bofs = p.op("vector", lambda E: E.scalar_tensor_tensor(
        aux_t, mu_t, -1.0, rstd_t, ALU.mult, ALU.mult), DV)
    p.wait("tensor", DV, t_bofs)
    tB2 = None
    for n in range(width // 512):
        p.op("tensor", _mm(P0[:, n * 512:(n + 1) * 512], onesf_row[0:1, 0:128],
                           rstd_t[0:1, n * 512:(n + 1) * 512], start=True, stop=True))
        tB2 = p.op("tensor", _mm(P1[:, n * 512:(n + 1) * 512], onesf_row[0:1, 0:128],
                                 aux_t[0:1, n * 512:(n + 1) * 512], start=True, stop=True), MM)
    return tB2




def _phaseC_moe(nc, stack, p, env):
    ent = stack.enter_context
    g = env
    stage, pipe = g["stage"], g["pipe"]
    x3b, x3_dram, outT = g["x3b"], g["x3_dram"], g["outT"]
    W1, W2, b1_s, b2s = g["W1"], g["W2"], g["b1_s"], g["b2s"]
    w8T, w_row = g["w8T"], g["w_row"]
    ones_col, onesf_row, eps_t = g["ones_col"], g["onesf_row"], g["eps_t"]
    sq16, yb16 = g["sq16"], g["yb16"]
    mu_t, rstd_t = g["mu_t"], g["rstd_t"]
    LD, CAST, MM, EV, DV, STS = g["LD"], g["CAST"], g["MM"], g["EV"], g["DV"], g["STS"]
    t_w8, st_x3 = g["t_w8"], g["st_x3"]

    OUTa = ent(nc.sbuf_tensor("OUTaC", [128, 8 * 1024], F32))
    h3 = ent(nc.sbuf_tensor("h3C", [128, 16 * 1024], BF16))
    w1b = ent(nc.sbuf_tensor("w1bC", [128, 8 * 2048], BF16))
    w2b = ent(nc.sbuf_tensor("w2bC", [128, 16 * 1024], BF16))
    wbc = ent(nc.sbuf_tensor("wbcC", [128, 1024], BF16))
    gel = [ent(nc.sbuf_tensor(f"gelC{i}", [128, 1024], BF16)) for i in range(2)]
    b2_sb = ent(nc.sbuf_tensor("b2sbC", [8, 1024], F32))
    outscr = stage
    stM = contextlib.ExitStack()
    Ph = [stM.enter_context(nc.psum_tensor(f"PhC{i}", [128, 1024], F32)) for i in range(2)]
    Po = [stM.enter_context(nc.psum_tensor(f"PoC{i}", [128, 1024], F32)) for i in range(2)]

    t_b2 = p.dma("sync", b2_sb[:], b2s[:, :], LD)
    st_out_last = 0
    prev_i_mm = [0]
    prev_ii_mm = [0]
    prev_gel = [None]
    prev_add = [None]
    prev_y_dv = [0]
    prev_wrow_mm = [g["tB2"]]
    prev_h3_ts = [None]
    for h2 in range(2):
        hoff = h2 * 1024
        p.wait("tensor", LD, t_b2)
        p.wait("tensor", DV, t_w8)
        if h2 == 1:
            p.wait("vector", STS, st_out_last)
            p.wait("tensor", DV, prev_add[0])
        seed_ts = []
        for M in range(8):
            ps = Po[M % 2]
            if M >= 2:
                p.wait("tensor", DV, seed_ts[M - 2])
            last = None
            for n in range(2):
                last = p.op("tensor", _mm(
                    ps[:, n * 512:(n + 1) * 512],
                    b2_sb[0:8, M * 128:(M + 1) * 128],
                    w8T[0:8, hoff + n * 512: hoff + (n + 1) * 512],
                    start=True, stop=True), MM)
            p.wait("vector", MM, last)
            seed_ts.append(p.op("vector", lambda E, M=M, ps=ps: E.tensor_copy(
                OUTa[:, M * 1024:(M + 1) * 1024], ps[:]), DV))

        for e in range(8):
            p.wait("sync", MM, prev_wrow_mm[0])
            t_wr = p.dma("sync", w_row[:], w8T[e:e + 1, :], LD)
            p.wait("tensor", LD, t_wr)
            if prev_gel[0] is not None:
                p.wait("tensor", EV, prev_gel[0])
            lastb = None
            for n in range(2):
                lastb = p.op("tensor", _mm(
                    Ph[0][:, n * 512:(n + 1) * 512], onesf_row[0:1, 0:128],
                    w_row[0:1, hoff + n * 512: hoff + (n + 1) * 512],
                    start=True, stop=True), MM)
            prev_wrow_mm[0] = lastb
            p.wait("vector", MM, lastb)
            t_wbc = p.op("vector", lambda E: E.tensor_copy(wbc[:], Ph[0][:]), DV)

            for slab in range(2):
                soff = slab * 2048
                p.wait("sync", MM, MM.n if FULLDRAIN else prev_i_mm[0])
                p.wait("sync", DV, prev_y_dv[0])
                t_w1 = None
                for fc in range(FC):
                    for g1 in range(2):
                        t_w1 = pipe.load_cast(
                            W1[e, fc * 128:(fc + 1) * 128,
                               soff + g1 * 1024: soff + (g1 + 1) * 1024],
                            w1b[:, fc * 2048 + g1 * 1024: fc * 2048 + (g1 + 1) * 1024],
                            1024, "gpsimd")
                t_w1done = t_w1
                p.wait("sync", MM, MM.n if FULLDRAIN else prev_ii_mm[0])
                t_w2 = None
                for hc in range(16):
                    t_w2 = pipe.load_cast(
                        W2[e, soff + hc * 128: soff + (hc + 1) * 128, 0:1024],
                        w2b[:, hc * 1024:(hc + 1) * 1024], 1024, "gpsimd")
                p.wait("tensor", CAST, t_w1done)
                p.wait("tensor", DV, t_wbc)
                if prev_gel[0] is not None:
                    p.wait("tensor", EV, prev_gel[0])
                gel_ts = {}
                h3_ts = {}
                for hc in range(16):
                    ph = Ph[hc % 2]
                    if hc >= 2:
                        p.wait("tensor", EV, gel_ts[hc - 2])
                    last = None
                    for fc in range(FC):
                        for n in range(2):
                            last = p.op("tensor", _mm(
                                ph[:, n * 512:(n + 1) * 512],
                                w1b[:, fc * 2048 + hc * 128: fc * 2048 + hc * 128 + 128],
                                x3b[:, fc * TOK + hoff + n * 512: fc * TOK + hoff + n * 512 + 512],
                                start=(fc == 0), stop=(fc == FC - 1)), MM)
                    last_i_mm = last
                    p.wait("scalar", MM, last)
                    if hc >= 2:
                        p.wait("scalar", DV, h3_ts[hc - 2])
                    elif prev_h3_ts[0] is not None and hc < 2:
                        p.wait("scalar", DV, prev_h3_ts[0])
                    bidx = e * 32 + slab * 16 + hc
                    gb = gel[hc % 2]
                    gel_ts[hc] = p.op("scalar", lambda E, ph=ph, bidx=bidx, gb=gb: E.activation(
                        gb[:], ph[:], AF.Gelu_apprx_tanh, bias=b1_s[:, bidx:bidx + 1]), EV)
                    p.wait("vector", EV, gel_ts[hc])
                    h3_ts[hc] = p.op("vector", lambda E, hc=hc, gb=gb: E.tensor_mul(
                        h3[:, hc * 1024:(hc + 1) * 1024], gb[:], wbc[:]), DV)
                h3_scaled_last = h3_ts[15]
                prev_h3_ts[0] = h3_ts[15]
                prev_gel[0] = gel_ts[15]
                prev_i_mm[0] = last_i_mm
                p.wait("tensor", CAST, t_w2)
                p.wait("tensor", DV, h3_scaled_last)
                if prev_add[0] is not None:
                    p.wait("tensor", DV, prev_add[0])
                add_ts = []
                for M in range(8):
                    po = Po[M % 2]
                    if M >= 2:
                        p.wait("tensor", DV, add_ts[M - 2])
                    elif e == 0 and slab == 0 and M < 2:
                        p.wait("tensor", DV, seed_ts[6 + M])
                    last = None
                    for hc in range(16):
                        for n in range(2):
                            last = p.op("tensor", _mm(
                                po[:, n * 512:(n + 1) * 512],
                                w2b[:, hc * 1024 + M * 128: hc * 1024 + M * 128 + 128],
                                h3[:, hc * 1024 + n * 512: hc * 1024 + n * 512 + 512],
                                start=(hc == 0), stop=(hc == 15)), MM)
                    p.wait("vector", MM, last)
                    add_ts.append(p.op("vector", lambda E, M=M, po=po: E.tensor_add(
                        OUTa[:, M * 1024:(M + 1) * 1024],
                        OUTa[:, M * 1024:(M + 1) * 1024], po[:]), DV))
                moe_add_last = add_ts[-1]
                prev_ii_mm[0] = last
                prev_add[0] = add_ts[-1]

        p.wait("tensor", EV, prev_gel[0])
        p.wait("tensor", DV, prev_add[0])
        stat_last = None
        y_ts = []
        for fc in range(FC):
            sb = stage[fc % 2]
            if fc >= 2:
                p.wait("sync", DV, y_ts[fc - 2])
            p.wait("sync", CAST, pipe.hist[-1])
            t_ld = p.dma("sync", sb[:], x3_dram[fc * 128:(fc + 1) * 128,
                                                hoff:hoff + 1024], LD)
            p.wait("vector", LD, t_ld)
            p.wait("vector", DV, moe_add_last)
            y_ts.append(p.op("vector", lambda E, fc=fc, sb=sb: E.tensor_add(
                OUTa[:, fc * 1024:(fc + 1) * 1024],
                OUTa[:, fc * 1024:(fc + 1) * 1024], sb[:]), DV))
            if stat_last is not None:
                p.wait("vector", MM, stat_last)
            t_yb = p.op("vector", lambda E, fc=fc: E.tensor_copy(
                yb16[:], OUTa[:, fc * 1024:(fc + 1) * 1024]), DV)
            p.wait("scalar", DV, t_yb)
            if stat_last is not None:
                p.wait("scalar", MM, stat_last)
            t_sq = p.op("scalar", lambda E, fc=fc: E.activation(
                sq16[:, 0:1024], OUTa[:, fc * 1024:(fc + 1) * 1024], AF.Square), EV)
            p.wait("tensor", DV, t_yb)
            p.wait("tensor", EV, t_sq)
            for n in range(2):
                p.op("tensor", _mm(Ph[0][0:1, n * 512:(n + 1) * 512], ones_col[:],
                                   yb16[:, n * 512:(n + 1) * 512],
                                   start=(fc == 0), stop=(fc == FC - 1)))
                stat_last = p.op("tensor", _mm(
                    Ph[1][0:1, n * 512:(n + 1) * 512], ones_col[:],
                    sq16[:, n * 512:(n + 1) * 512],
                    start=(fc == 0), stop=(fc == FC - 1)), MM)
        tB2 = emit_ln_scalar2(p, Ph[0], Ph[1], mu_t, w_row[0:1, 0:1024], rstd_t,
                              eps_t, onesf_row, MM, EV, DV, stat_last, width=1024)
        prev_wrow_mm[0] = tB2
        p.wait("vector", MM, tB2)
        st_ts = []
        for fc in range(FC):
            if fc >= 2:
                p.wait("vector", STS, st_ts[fc - 2])
            p.op("vector", lambda E, fc=fc: E.tensor_mul(
                outscr[fc % 2][:], OUTa[:, fc * 1024:(fc + 1) * 1024], Ph[0][:]))
            t_ap2 = p.op("vector", lambda E, fc=fc: E.tensor_add(
                outscr[fc % 2][:], outscr[fc % 2][:], Ph[1][:]), DV)
            p.wait("sync", DV, t_ap2)
            st_ts.append(p.dma("sync", outT[fc * 128:(fc + 1) * 128, hoff:hoff + 1024],
                               outscr[fc % 2][:], STS))
        st_out_last = st_ts[-1]
        prev_y_dv[0] = t_ap2
        p.wait("tensor", DV, t_ap2)
    p.wait("sync", STS, st_out_last)
    stM.close()
    p.finalize()




def stage_phaseC(x2T_cores, inp):
    wqkvT1 = _cc(np.concatenate([inp["L1_attn_Wq"], inp["L1_attn_Wk"], inp["L1_attn_Wv"]], 0).T)
    bq1_pp = _pp(np.concatenate([inp["L1_attn_bq"], inp["L1_attn_bk"], inp["L1_attn_bv"]], 0))
    woT1 = _cc(np.asarray(inp["L1_attn_Wo"]).T)
    bo1_pp = _pp(inp["L1_attn_bo"])
    Wg = _cc(inp["L1_ffn_Wg"])
    bg_row = _cc(np.asarray(inp["L1_ffn_bg"])[None, :])
    W1 = _cc(inp["L1_ffn_W1"])
    b1 = np.asarray(inp["L1_ffn_b1"], np.float32)
    b1_pp = _cc(np.concatenate([b1[e].reshape(32, 128).T for e in range(8)], axis=1))
    W2 = _cc(inp["L1_ffn_W2"])
    b2s = _cc(inp["L1_ffn_b2"])
    maps = []
    for b in range(NCORES):
        maps.append(dict(x2T=_cc(x2T_cores[b]), wqkvT1=wqkvT1, bq1_pp=bq1_pp,
                         woT1=woT1, bo1_pp=bo1_pp, Wg=Wg, bg_row=bg_row,
                         W1=W1, b1_pp=b1_pp, W2=W2, b2s=b2s))
    return maps


def kernel(**inputs):
    mapsA = stage_phaseA(inputs)
    ncA = build_phaseA()
    resA = run_bass_kernel_spmd(ncA, mapsA, core_ids=list(range(NCORES)))
    x1T = [np.asarray(resA.results[b]["x1T"], np.float32) for b in range(NCORES)]
    mapsB = stage_phaseB(x1T, inputs)
    ncB = build_phaseB()
    resB = run_bass_kernel_spmd(ncB, mapsB, core_ids=list(range(NCORES)))
    x2T = unstage_phaseB(resB.results)
    mapsC = stage_phaseC(x2T, inputs)
    ncC = build_phaseC()
    resC = run_bass_kernel_spmd(ncC, mapsC, core_ids=list(range(NCORES)))
    out = np.stack([np.asarray(resC.results[b]["outT"], np.float32).T
                    for b in range(NCORES)], axis=0)
    return out
